# version 18
# baseline (speedup 1.0000x reference)
import numpy as np
import ml_dtypes
import concourse.bass as bass
import concourse.mybir as mybir
from concourse.bass_utils import run_bass_kernel_spmd
from contextlib import ExitStack


def _install_neff_cache():
    try:
        import os as _os, hashlib as _hl, pathlib as _pl
        import concourse.bass2jax as _b2j
        import concourse.bass_utils as _bu
        if getattr(_bu, "_neff_cache_installed", False):
            return
        _orig = _bu.compile_bir_kernel
        _cdir = _pl.Path(_os.path.expanduser("~")) / ".bass_neff_cache"

        def _cached(bir_json, tmpdir, neff_name="file.neff"):
            cpath = None
            try:
                flags = _os.environ.get("AXON_NCC_FLAGS", "")
                key = _hl.sha256(
                    bytes(bir_json) + b"|" + flags.encode() + b"|" + neff_name.encode()
                ).hexdigest()
                cpath = _cdir / (key + ".neff")
                if cpath.is_file():
                    out = _os.path.join(tmpdir, neff_name)
                    with open(cpath, "rb") as f, open(out, "wb") as g:
                        g.write(f.read())
                    return out
            except Exception:
                cpath = None
            neff = _orig(bir_json, tmpdir, neff_name=neff_name)
            try:
                if cpath is not None:
                    _cdir.mkdir(parents=True, exist_ok=True)
                    tmp = str(cpath) + f".tmp{_os.getpid()}"
                    with open(neff, "rb") as f, open(tmp, "wb") as g:
                        g.write(f.read())
                    _os.replace(tmp, str(cpath))
            except Exception:
                pass
            return neff

        _bu.compile_bir_kernel = _cached
        _b2j.compile_bir_kernel = _cached
        _bu._neff_cache_installed = True
    except Exception:
        pass


_install_neff_cache()

F32 = mybir.dt.float32
BF16 = mybir.dt.bfloat16
U8 = mybir.dt.uint8
AF = mybir.ActivationFunctionType
ALU = mybir.AluOpType

D = 2048
S = 4096
H = 16
KVH = 4
HID = 5632
DC = D // 128
HC = HID // 128
TOWN = 2048
NG = 2
GT = 1024
NB0 = 8
B0T = 512
NOWN0 = 4
LN_EPS = 1e-5
ATT_EPS = 1e-6

ENGS = ("sync", "gpsimd", "tensor", "vector", "scalar")
ENGSEM = {"tensor": "SP", "vector": "SV", "scalar": "SA"}
DMASEMS = ("SSU", "SX0", "SX1", "SX2", "SX3", "SY0", "SY1", "SY2", "SY3", "SRW",
           "GX0", "GX1", "GW", "GA0", "GA1", "GQ0", "GQ1", "GB0", "GB1", "GB2", "GB3",
           "GG0", "GG1", "GU0", "GU1", "GO0", "GO1", "GD0", "GD1", "GD2", "GR", "GC", "GK")
ALLSEMS = ("SP", "SV", "SA") + DMASEMS


class Res:
    __slots__ = ("name", "ready", "readers", "pending_on")

    def __init__(self, name):
        self.name = name
        self.ready = []
        self.readers = []
        self.pending_on = set()


class Plan:
    def __init__(self):
        self.th = {e: [] for e in ENGS}
        self.cnt = {s: 0 for s in ALLSEMS}
        self.mark = {e: {s: 0 for s in ALLSEMS} for e in ENGS}
        self.pending = {e: [] for e in ENGS}
        self.barrier_evs = []
        self.last_ev = {}

    def op(self, eng, build, reads=(), writes=(), inc=None, inc_by=1, extra=()):
        if inc is None and eng in ("vector", "scalar"):
            inc = ENGSEM[eng]
        waits = list(extra)
        for r in reads:
            waits += r.ready
        for r in writes:
            for other in r.pending_on:
                if other != eng:
                    raise RuntimeError(f"write to {r.name} on {eng} while touches pending on {other}")
            waits += r.ready + r.readers
        fw = []
        for (s, v) in waits:
            if eng == "tensor" and s == "SP":
                continue
            if v > self.mark[eng][s]:
                self.mark[eng][s] = v
                fw.append((s, v))
        ev = None
        if inc is not None:
            self.cnt[inc] += inc_by
            ev = (inc, self.cnt[inc])

        def thunk(E, sems, fw=fw, ev=ev, build=build, inc_by=inc_by):
            for (s, v) in fw:
                E.wait_ge(sems[s], v)
            ins = build(E)
            if ev is not None:
                ins.then_inc(sems[ev[0]], inc_by)
        self.th[eng].append(thunk)
        for r in reads:
            self.pending[eng].append((r, "r"))
            r.pending_on.add(eng)
        for r in writes:
            self.pending[eng].append((r, "w"))
            r.pending_on.add(eng)
        if ev is not None:
            self.last_ev[eng] = ev
            for (r, kind) in self.pending[eng]:
                if kind == "w":
                    r.ready = [ev]
                    r.readers = []
                else:
                    r.readers.append(ev)
                r.pending_on.discard(eng)
            self.pending[eng] = []
        return ev

    def res(self, name):
        r = Res(name)
        r.ready = list(self.barrier_evs)
        return r

    def dma(self, eng, sem, out_ap, in_ap, writes=(), reads=()):
        extra = [(s, v) for (s, v) in self.barrier_evs
                 if not (s.startswith("SX") or s.startswith("SY"))]
        return self.op(eng, lambda E: E.dma_start(out_ap, in_ap), reads=reads,
                       writes=writes, inc=sem, inc_by=16, extra=extra)

    def barrier(self):
        assert not any(self.pending[e] for e in ("tensor", "vector", "scalar")), \
            "barrier with unresolved pending ops"
        self.barrier_evs = [self.last_ev[e] for e in ("tensor", "vector", "scalar")
                            if e in self.last_ev]
        self.barrier_evs += [(s, self.cnt[s]) for s in DMASEMS if self.cnt[s] > 0]
        return self.barrier_evs


def build_nc():
    nc = bass.Bass("TRN2", target_bir_lowering=False)
    p = Plan()

    xblk = nc.dram_tensor("xblk", [NOWN0, 128, 8192], F32, kind="ExternalInput").ap()
    xtok = nc.dram_tensor("xtok", [TOWN, D], F32, kind="ExternalInput").ap()
    wq = nc.dram_tensor("wq", [H, 128, D], F32, kind="ExternalInput").ap()
    wkv = nc.dram_tensor("wkv", [128, DC, 1024], F32, kind="ExternalInput").ap()
    bkv = nc.dram_tensor("bkv", [1, 1024], F32, kind="ExternalInput").ap()
    bq = nc.dram_tensor("bq", [128, H], F32, kind="ExternalInput").ap()
    wg = nc.dram_tensor("wg", [HC, 128, D], U8, kind="ExternalInput").ap()
    wu = nc.dram_tensor("wu", [HC, 128, D], U8, kind="ExternalInput").ap()
    sgi = nc.dram_tensor("sgi", [128, HC], F32, kind="ExternalInput").ap()
    sui = nc.dram_tensor("sui", [128, HC], F32, kind="ExternalInput").ap()
    wo = nc.dram_tensor("wo", [4, 128, H * 512], F32, kind="ExternalInput").ap()
    wd = nc.dram_tensor("wd", [4, 128, HC * 512], U8, kind="ExternalInput").ap()
    sd = nc.dram_tensor("sd", [1, D], F32, kind="ExternalInput").ap()
    y = nc.dram_tensor("y", [TOWN, D], F32, kind="ExternalOutput").ap()
    cc_in = nc.dram_tensor("cc_in", [128, 528], F32).ap()
    cc_out = nc.dram_tensor("cc_out", [128, 528], F32).ap()

    with ExitStack() as top:
        sbuf = lambda n, s, d: top.enter_context(nc.sbuf_tensor(n, s, d))
        sems = {n: top.enter_context(nc.semaphore(n)) for n in ALLSEMS}

        hidden = sbuf("hidden", [128, HC * GT], BF16)
        attn = sbuf("attn", [128, H * GT], BF16)
        xng = sbuf("xng", [128, DC * GT], BF16)
        war = sbuf("war", [128, 12288], BF16)
        warf = sbuf("warf", [128, 2048], F32)
        stat = sbuf("stat", [128, 2048], BF16)

        cst = sbuf("cst", [128, 8], F32)
        CVALS = [0.0, 1.0, -1.0, -0.5, LN_EPS, ATT_EPS]
        for i, v in enumerate(CVALS):
            nc.const_aps.register(F32, v, cst[:, i:i + 1])
        ones = sbuf("ones", [128, 128], BF16)
        m128 = sbuf("m128", [2, 128], BF16)
        kvb = sbuf("kvb", [128, KVH * 128], BF16)
        ksb = sbuf("ksb", [128, KVH], BF16)
        bqb = sbuf("bqb", [128, H], F32)
        bqn = sbuf("bqn", [128, H], F32)
        sgb = sbuf("sgb", [128, HC], F32)
        sub = sbuf("sub", [128, HC], F32)
        mrsave = sbuf("mrsave", [128, 4096], BF16)
        scr = sbuf("scr", [128, 512], BF16)
        rowf = sbuf("rowf", [1, 1024], F32)
        rowb = sbuf("rowb", [2, 2048], BF16)
        lnrow = rowf[0:1, 0:1024]
        csrow = rowf[0:1, 0:1024]
        cs2 = rowb[0:2, 0:1024]
        rec = rowb[0:1, 1024:2048]
        sdrow = rowb[0:1, 0:2048]

        pb = [top.enter_context(nc.psum_tensor(f"pb{i}", [128, 512], F32)) for i in range(8)]
        bank = [Res(f"bank{i}") for i in range(8)]

        R = {n: Res(n) for n in ("cst", "ones", "m128", "kvb", "ksb", "bqb", "bqn",
                                 "sgb", "sub", "mrsave", "scr", "rowf", "rowb")}
        RD = {n: Res(n) for n in ("xt0", "xt1", "xt2", "xt3",
                                  "oev0", "oev1", "oev2", "oev3")}

        p.dma("sync", "SSU", bqb[:, :], bq, writes=[R["bqb"]])
        p.dma("sync", "SSU", sgb[:, :], sgi, writes=[R["sgb"]])
        p.dma("sync", "SSU", sub[:, :], sui, writes=[R["sub"]])
        for _r in (R["bqb"], R["sgb"], R["sub"]):
            _r.ready = [("SSU", p.cnt["SSU"])]
        for i, v in enumerate(CVALS):
            p.op("vector", lambda E, i=i, v=v: E.memset(cst[:, i:i + 1], v), writes=[R["cst"]])
        p.op("vector", lambda E: E.memset(ones[:, :], 1.0), writes=[R["ones"]])
        p.op("vector", lambda E: E.memset(m128[:, :], -128.0), writes=[R["m128"]])
        p.op("vector", lambda E: E.tensor_scalar_mul(bqn[:, :], bqb[:, :], -1.0),
             reads=[R["bqb"]], writes=[R["bqn"]])
        consts_ev = [("SV", p.cnt["SV"])]
        p.dma("gpsimd", "GR", sdrow, sd, writes=[R["rowb"]])

        wkvb = hidden[:, 0:16384]
        xsq = hidden[:, 32768:40960]
        skb = hidden[:, 40960:41984]
        vab = hidden[:, 41984:43040]
        bkvb = hidden[0:1, 43072:44096]
        ftmp = warf[:, 0:1024]
        varb = warf[:, 1024:2048]
        r0 = {n: p.res(n) for n in ("wkvb", "xsq", "xnb0", "xnb1",
                                    "skb0", "skb1", "vab0", "vab1", "stat", "ftmp",
                                    "varb", "bkvb")}
        for par in range(2):
            for q in range(4):
                r0[f"xbf{par}q{q}"] = p.res(f"xbf{par}q{q}")

        for par in range(2):
            p.op("vector", lambda E, par=par: E.memset(
                vab.rearrange("p (q g c) -> p q g c", q=2, g=4)[:, par, :, 128:129], 1.0),
                writes=[r0[f"vab{par}"]])

        def emit_w0_loads():
            p.dma("gpsimd", "GK", bkvb, bkv, writes=[r0["bkvb"]])
            wkv_evs = []
            for i, sem in enumerate(("GW", "GG0", "GU0", "GG1")):
                rq = Res(f"wkvq{i}")
                p.dma("gpsimd", sem,
                      wkvb.rearrange("p (c f) -> p c f", c=DC)[:, 4 * i:4 * i + 4, :],
                      wkv[:, 4 * i:4 * i + 4, :], writes=[rq])
                wkv_evs += rq.ready
            r0["wkvb"].ready = wkv_evs
            for h in range(2):
                p.dma("gpsimd", f"GQ{h}", war[:, h * 2048:(h + 1) * 2048], wq[h],
                      writes=[wqres0[h]])
        wqres0 = [p.res(f"g0wq{i}") for i in range(4)]

        xnres = {0: [p.res("xng0h0"), p.res("xng0h1")]}

        def emit_xn_build(g, dcs=range(DC)):
            if g not in xnres:
                xnres[g] = [p.res(f"xng{g}h0"), p.res(f"xng{g}h1")]
                for h2 in range(2):
                    p.dma("gpsimd", f"GA{h2}",
                          xng[:, h2 * 8192:(h2 + 1) * 8192],
                          xblk[g * 2 + h2], writes=[xnres[g][h2]])
            for h2 in range(2):
                rx = xnres[g][h2]
                blk = (g * GT + h2 * 512) // 512
                m_ap = mrsave[:, blk * 1024: blk * 1024 + 512]
                r_ap = mrsave[:, blk * 1024 + 512: blk * 1024 + 1024]
                for dc in dcs:
                    sl = xng[:, h2 * 8192 + dc * 512: h2 * 8192 + (dc + 1) * 512]
                    p.op("vector", lambda E, sl=sl, m_ap=m_ap: E.tensor_tensor(
                        sl, sl, m_ap, ALU.subtract),
                        reads=[rx, R["mrsave"]], writes=[rx])
                    p.op("vector", lambda E, sl=sl, r_ap=r_ap: E.tensor_tensor(
                        sl, sl, r_ap, ALU.mult),
                        reads=[R["mrsave"]], writes=[rx])

        st_idx = 0
        pend_kv = []

        def flush_kv():
            while pend_kv:
                sk_ap, va_base, rsk, rva, sti = pend_kv.pop(0)
                for g4 in range(4):
                    bnk = bank[6 + g4 // 2]
                    pp = pb[6 + g4 // 2]
                    col = (g4 % 2) * 256
                    p.op("tensor", lambda E, g4=g4, sk_ap=sk_ap, va_base=va_base, pp=pp, col=col,
                         first=(sti == 0 and g4 % 2 == 0), last=(sti == 31): E.matmul(
                             pp[:, col:col + 129], sk_ap[:, g4 * 128:(g4 + 1) * 128],
                             vab[:, va_base + g4 * 132: va_base + g4 * 132 + 129],
                             start=first, stop=last, skip_group_check=True),
                        reads=[rsk, rva], writes=[bnk],
                        inc=("SP" if g4 == 3 else None))

        def emit_xdma(b):
            par = b % 2
            xb = hidden[:, 16384 + par * 8192: 16384 + (par + 1) * 8192]
            for q in range(4):
                p.dma("gpsimd", (f"GX{par}", f"GA{par}", f"GB{par}", f"GB{par + 2}")[q],
                      xb[:, q * 2048:(q + 1) * 2048],
                      xblk[b][:, q * 2048:(q + 1) * 2048],
                      writes=[r0[f"xbf{par}q{q}"]])

        def emit_square(b):
            par = b % 2
            xb = hidden[:, 16384 + par * 8192: 16384 + (par + 1) * 8192]
            for q in range(4):
                p.op("vector", lambda E, xb=xb, q=q: E.tensor_tensor(
                    xsq[:, q * 2048:(q + 1) * 2048], xb[:, q * 2048:(q + 1) * 2048],
                    xb[:, q * 2048:(q + 1) * 2048], ALU.mult),
                    reads=[r0[f"xbf{par}q{q}"]], writes=[r0["xsq"]], extra=consts_ev)

        def stat_aps(b):
            return (mrsave[:, b * 1024: b * 1024 + 512],
                    mrsave[:, b * 1024 + 512: b * 1024 + 1024], R["mrsave"])

        def emit_s1(b):
            par = b % 2
            xb = hidden[:, 16384 + par * 8192: 16384 + (par + 1) * 8192]
            for dc in range(DC):
                p.op("tensor", lambda E, dc=dc, xb=xb: E.matmul(
                    pb[4][:, :], ones[:, :], xb[:, dc * 512:(dc + 1) * 512],
                    start=(dc == 0), stop=(dc == DC - 1)),
                    reads=[r0[f"xbf{par}q{dc // 4}"], R["ones"]], writes=[bank[4]])

        def emit_s2(b):
            for dc in range(DC):
                p.op("tensor", lambda E, dc=dc: E.matmul(
                    pb[5][:, :], ones[:, :], xsq[:, dc * 512:(dc + 1) * 512],
                    start=(dc == 0), stop=(dc == DC - 1)),
                    reads=[r0["xsq"], R["ones"]], writes=[bank[5]],
                    inc=("SP" if dc == DC - 1 else None))

        def emit_mr_xn(b):
            par = b % 2
            xb = hidden[:, 16384 + par * 8192: 16384 + (par + 1) * 8192]
            if b < 2:
                xn, rxnw = xng[:, b * 8192:(b + 1) * 8192], xnres[0][b]
            else:
                xn, rxnw = attn[:, par * 8192:(par + 1) * 8192], r0[f"xnb{par}"]
            m_ap, r_ap, rmr = stat_aps(b)
            p.op("vector", lambda E, m_ap=m_ap: E.tensor_scalar_mul(m_ap, pb[4][:, :], 1.0 / 2048.0),
                 reads=[bank[4]], writes=[rmr])
            p.op("scalar", lambda E, m_ap=m_ap: E.activation(varb[:, 0:512], m_ap, AF.Square),
                 reads=[rmr], writes=[r0["varb"]])
            p.op("vector", lambda E: E.scalar_tensor_tensor(
                varb[:, 512:1024], pb[5][:, :], 1.0 / 2048.0, varb[:, 0:512], ALU.mult, ALU.subtract),
                reads=[bank[5], r0["varb"]], writes=[r0["varb"]])
            p.op("scalar", lambda E: E.activation(varb[:, 0:512], varb[:, 512:1024], AF.Ln, bias=LN_EPS),
                 reads=[r0["varb"]], writes=[r0["varb"]])
            p.op("scalar", lambda E, r_ap=r_ap: E.activation(r_ap, varb[:, 0:512], AF.Exp, scale=-0.5),
                 reads=[r0["varb"]], writes=[rmr])
            for dc in range(DC):
                p.op("vector", lambda E, dc=dc, xb=xb, m_ap=m_ap: E.tensor_tensor(
                    scr[:, :], xb[:, dc * 512:(dc + 1) * 512], m_ap, ALU.subtract),
                    reads=[r0[f"xbf{par}q{dc // 4}"], rmr], writes=[R["scr"]])
                p.op("vector", lambda E, dc=dc, xn=xn, r_ap=r_ap: E.tensor_tensor(
                    xn[:, dc * 512:(dc + 1) * 512], scr[:, :], r_ap, ALU.mult),
                    reads=[R["scr"], rmr], writes=[rxnw])

        NOWN = 4
        emit_xdma(0)
        emit_xdma(1)
        emit_w0_loads()
        emit_square(0)
        emit_s1(0)
        emit_s2(0)
        emit_mr_xn(0)
        emit_square(1)

        for b in range(NOWN):
            par = b % 2
            xb = hidden[:, 16384 + par * 8192: 16384 + (par + 1) * 8192]
            if b < 2:
                xn, rxn = xng[:, b * 8192:(b + 1) * 8192], xnres[0][b]
            else:
                xn, rxn = attn[:, par * 8192:(par + 1) * 8192], r0[f"xnb{par}"]
            if b + 2 < NOWN:
                emit_xdma(b + 2)
            if b + 1 < NOWN:
                emit_s1(b + 1)
            for s in range(4):
                ps = s % 2
                bk_, bv_ = (bank[0], bank[1]) if ps == 0 else (bank[2], bank[3])
                pk, pv = (pb[0], pb[1]) if ps == 0 else (pb[2], pb[3])
                rsk, rva = r0[f"skb{ps}"], r0[f"vab{ps}"]
                sk_ap = skb[:, ps * 512:(ps + 1) * 512]
                va_base = ps * 528
                p.op("tensor", lambda E, pk=pk: E.matmul(
                    pk[:, :], ones[0:1, :], bkvb[0:1, 0:512], start=True, stop=False),
                    reads=[r0["bkvb"], R["ones"]], writes=[bk_])
                p.op("tensor", lambda E, pv=pv: E.matmul(
                    pv[:, :], ones[0:1, :], bkvb[0:1, 512:1024], start=True, stop=False),
                    reads=[r0["bkvb"], R["ones"]], writes=[bv_])
                for dc in range(DC):
                    lhs = xn[:, dc * 512 + s * 128: dc * 512 + (s + 1) * 128]
                    p.op("tensor", lambda E, pk=pk, dc=dc, lhs=lhs: E.matmul(
                        pk[:, :], lhs, wkvb[:, dc * 1024: dc * 1024 + 512],
                        start=False, stop=(dc == DC - 1)),
                        reads=[rxn, r0["wkvb"]], writes=[bk_])
                    p.op("tensor", lambda E, pv=pv, dc=dc, lhs=lhs: E.matmul(
                        pv[:, :], lhs, wkvb[:, dc * 1024 + 512: (dc + 1) * 1024],
                        start=False, stop=(dc == DC - 1)),
                        reads=[rxn, r0["wkvb"]], writes=[bv_],
                        inc=("SP" if dc == DC - 1 else None))
                if s == 0 and b + 1 < NOWN:
                    emit_s2(b + 1)
                    emit_mr_xn(b + 1)
                flush_kv()
                p.op("scalar", lambda E, pk=pk: E.activation(varb[:, 0:512], pk[:, :], AF.Relu),
                     reads=[bk_], writes=[r0["varb"]])
                p.op("scalar", lambda E, pk=pk: E.activation(ftmp[:, 0:512], pk[:, :], AF.Relu, scale=-1.0),
                     reads=[bk_], writes=[r0["ftmp"]])
                p.op("scalar", lambda E: E.activation(ftmp[:, 512:1024], ftmp[:, 0:512], AF.Exp, scale=-1.0),
                     reads=[r0["ftmp"]], writes=[r0["ftmp"]])
                p.op("vector", lambda E, sk_ap=sk_ap: E.tensor_tensor(
                    sk_ap, varb[:, 0:512], ftmp[:, 512:1024], ALU.add),
                    reads=[r0["varb"], r0["ftmp"]], writes=[rsk])
                p.op("vector", lambda E, pv=pv, va_base=va_base: E.tensor_copy(
                    vab[:, va_base:va_base + 528].rearrange("p (g c) -> p g c", g=4)[:, :, 0:128],
                    pv[:, :].rearrange("p (g c) -> p g c", g=4)),
                    reads=[bv_], writes=[rva])
                pend_kv.append((sk_ap, va_base, rsk, rva, st_idx))
                st_idx += 1
                if s == 2 and b + 2 < NOWN:
                    emit_square(b + 2)
        flush_kv()
        kvp = warf[:, 0:528]
        rkvp = Res("kvp")
        p.op("vector", lambda E: E.memset(kvp[:, 0:528], 0.0),
             reads=[r0["ftmp"]], writes=[rkvp])
        for g4 in range(4):
            pp = pb[6 + g4 // 2]
            col = (g4 % 2) * 256
            p.op("vector", lambda E, pp=pp, col=col, g4=g4: E.tensor_copy(
                kvp[:, g4 * 132: g4 * 132 + 129], pp[:, col:col + 129]),
                reads=[bank[6 + g4 // 2]], writes=[rkvp])
        rcc_in = Res("cc_in")
        p.dma("sync", "SRW", cc_in, kvp[:, 0:528], reads=[rkvp], writes=[rcc_in])
        rcc = Res("cc")
        p.op("gpsimd", lambda E: E.collective_compute(
            "AllReduce", ALU.add, [[0, 1], [2, 3], [4, 5], [6, 7]],
            [cc_in.opt()], [cc_out.opt()]),
            reads=[rcc_in], writes=[rcc], inc="GC", inc_by=1)
        p.dma("sync", "SSU", kvp[:, 0:528], cc_out, writes=[rkvp], reads=[rcc])

        def emit_kv_unpack():
            for g4 in range(4):
                p.op("vector", lambda E, g4=g4: E.tensor_copy(
                    kvb[:, g4 * 128:(g4 + 1) * 128], kvp[:, g4 * 132: g4 * 132 + 128]),
                    reads=[rkvp], writes=[R["kvb"]])
                p.op("vector", lambda E, g4=g4: E.tensor_copy(
                    ksb[:, g4:g4 + 1], kvp[:, g4 * 132 + 128: g4 * 132 + 129]),
                    reads=[rkvp], writes=[R["ksb"]])

        sdbc = stat[:, 0:2048]
        rsdbc = Res("sdbc")
        for qq in range(4):
            p.op("tensor", lambda E, qq=qq: E.matmul(
                pb[4][:, :], ones[0:1, :], sdrow[0:1, qq * 512:(qq + 1) * 512]),
                reads=[R["rowb"], R["ones"]], writes=[bank[4]], inc="SP")
            p.op("vector", lambda E, qq=qq: E.tensor_copy(
                sdbc[:, qq * 512:(qq + 1) * 512], pb[4][:, :]),
                reads=[bank[4]], writes=[rsdbc])

        def join(*ress):
            evs = []
            for r in ress:
                evs += r.ready + r.readers
            return evs

        for g in range(NG):
            gt0 = g * GT
            rxng = xnres[g]

            def atm_ap(w, pa):
                off = 16384 + pa * 2048 + w * 1024
                return hidden[:, off:off + 1024]
            if g == 0:
                def mkres(name, evs):
                    r = Res(name)
                    r.ready = list(evs)
                    return r
                sq_evs = join(r0["wkvb"])
                atm_evs = join(*[r0[f"xbf0q{q}"] for q in range(4)])
                rsq = [mkres(f"sq{h}", sq_evs) for h in range(H)]
                ratm = [mkres(f"atm{pa}", atm_evs) for pa in range(2)]
                rattn = mkres("attn", join(r0["xnb0"], r0["xnb1"]))
                rhid = mkres("hidden", [])
                rwq, npre = wqres0, 2
            else:
                rsq = [p.res(f"sq{h}") for h in range(H)]
                ratm = [p.res(f"atm{pa}") for pa in range(2)]
                rattn = p.res("attn")
                rhid = p.res("hidden")
                rwq, npre = [p.res(f"g{g}wq{i}") for i in range(4)], 0

            def emit_q_half(h, half):
                hp = h % 2
                rw = rwq[h % 4]
                wqb = war[:, (h % 4) * 2048: (h % 4 + 1) * 2048]
                if half == 0 and h >= npre:
                    p.dma("gpsimd", f"GQ{hp}", wqb, wq[h], writes=[rw])
                b0, b1 = (bank[0], bank[1]) if hp == 0 else (bank[2], bank[3])
                q0, q1 = (pb[0], pb[1]) if hp == 0 else (pb[2], pb[3])
                sq = hidden[:, h * 1024:(h + 1) * 1024]
                atm1, atm2 = atm_ap(0, hp), atm_ap(1, hp)
                qq, bnk = (q0, b0) if half == 0 else (q1, b1)
                for dc in range(DC):
                    lhs = wqb[:, dc * 128:(dc + 1) * 128]
                    p.op("tensor", lambda E, qq=qq, dc=dc, lhs=lhs, half=half: E.matmul(
                        qq[:, :], lhs, xng[:, half * 8192 + dc * 512: half * 8192 + (dc + 1) * 512],
                        start=(dc == 0), stop=(dc == DC - 1)),
                        reads=[rw] + rxng, writes=[bnk],
                        inc=("SP" if dc == DC - 1 else None))
                hsl = slice(half * 512, (half + 1) * 512)
                p.op("scalar", lambda E, qq=qq, hsl=hsl, h=h, atm1=atm1: E.activation(
                    atm1[:, hsl], qq[:, :], AF.Relu, bias=bqb[:, h:h + 1]),
                    reads=[bnk, R["bqb"]], writes=[ratm[hp]])
                p.op("scalar", lambda E, qq=qq, hsl=hsl, h=h, atm2=atm2: E.activation(
                    atm2[:, hsl], qq[:, :], AF.Relu, scale=-1.0, bias=bqn[:, h:h + 1]),
                    reads=[bnk, R["bqn"]], writes=[ratm[hp]])
                p.op("scalar", lambda E, sq=sq, hsl=hsl, atm2=atm2: E.activation(
                    sq[:, hsl], atm2[:, hsl], AF.Exp, scale=-1.0),
                    reads=[ratm[hp]], writes=[rsq[h]])
                p.op("vector", lambda E, sq=sq, hsl=hsl, atm1=atm1: E.tensor_tensor(
                    sq[:, hsl], sq[:, hsl], atm1[:, hsl], ALU.add),
                    reads=[ratm[hp]], writes=[rsq[h]])

            def emit_qkv_half(h, half):
                hp = h % 2
                g4 = h // 4
                sq = hidden[:, h * 1024:(h + 1) * 1024]
                kb = (bank[4], bank[5]) if hp == 0 else (bank[6], bank[7])
                KP = (pb[4], pb[5]) if hp == 0 else (pb[6], pb[7])
                qb = (bank[0], bank[1]) if hp == 0 else (bank[2], bank[3])
                qp = (pb[0], pb[1]) if hp == 0 else (pb[2], pb[3])
                hsl = slice(half * 512, (half + 1) * 512)
                p.op("tensor", lambda E, K=KP[half], g4=g4, sq=sq, hsl=hsl: E.matmul(
                    K[:, :], kvb[:, g4 * 128:(g4 + 1) * 128], sq[:, hsl]),
                    reads=[R["kvb"], rsq[h]], writes=[kb[half]])
                p.op("tensor", lambda E, q=qp[half], g4=g4, sq=sq, hsl=hsl: E.matmul(
                    q[0:1, :], ksb[:, g4:g4 + 1], sq[:, hsl]),
                    reads=[R["ksb"], rsq[h]], writes=[qb[half]], inc="SP")
                p.op("scalar", lambda E, q=qp[half], hsl=hsl: E.activation(
                    lnrow[0:1, hsl], q[0:1, :], AF.Ln, bias=ATT_EPS),
                    reads=[qb[half]], writes=[R["rowf"]])
                if half == 1:
                    p.op("scalar", lambda E: E.activation(rec, lnrow, AF.Exp, scale=-1.0),
                         reads=[R["rowf"]], writes=[R["rowb"]])

            def emit_rb_attn(h):
                hp = h % 2
                b0, b1 = (bank[0], bank[1]) if hp == 0 else (bank[2], bank[3])
                q0, q1 = (pb[0], pb[1]) if hp == 0 else (pb[2], pb[3])
                k0, k1 = (bank[4], bank[5]) if hp == 0 else (bank[6], bank[7])
                K0, K1 = (pb[4], pb[5]) if hp == 0 else (pb[6], pb[7])
                sq = hidden[:, h * 1024:(h + 1) * 1024]
                p.op("tensor", lambda E, q0=q0: E.matmul(
                    q0[:, :], ones[0:1, :], rec[0:1, 0:512]),
                    reads=[R["rowb"], R["ones"]], writes=[b0])
                p.op("tensor", lambda E, q1=q1: E.matmul(
                    q1[:, :], ones[0:1, :], rec[0:1, 512:1024]),
                    reads=[R["rowb"], R["ones"]], writes=[b1], inc="SP")
                p.op("vector", lambda E, K0=K0, sq=sq: E.tensor_copy(sq[:, 0:512], K0[:, :]),
                     reads=[k0], writes=[rsq[h]])
                p.op("vector", lambda E, K1=K1, sq=sq: E.tensor_copy(sq[:, 512:1024], K1[:, :]),
                     reads=[k1], writes=[rsq[h]])
                p.op("vector", lambda E, q0=q0, sq=sq, h=h: E.tensor_tensor(
                    attn[:, h * GT: h * GT + 512], sq[:, 0:512], q0[:, :], ALU.mult),
                    reads=[rsq[h], b0], writes=[rattn])
                p.op("vector", lambda E, q1=q1, sq=sq, h=h: E.tensor_tensor(
                    attn[:, h * GT + 512: (h + 1) * GT], sq[:, 512:1024], q1[:, :], ALU.mult),
                    reads=[rsq[h], b1], writes=[rattn])

            QKD = 7
            for step in range(H + QKD):
                if step >= QKD:
                    h2 = step - QKD
                    emit_qkv_half(h2, 0)
                    emit_qkv_half(h2, 1)
                    emit_rb_attn(h2)
                if step < H:
                    emit_q_half(step, 0)
                    emit_q_half(step, 1)
                if g == 0 and step == 5:
                    emit_kv_unpack()
            p.barrier()

            wgu = war[:, 0:8192]
            gsil = war[:, 8192:9216]
            gup = war[:, 9216:10240]
            rf = {n: p.res(n) for n in ("wg0", "wg1", "wu0", "wu1", "gsil", "gup")}

            for dc in range(DC):
                p.op("tensor", lambda E, dc=dc: E.matmul(
                    pb[0][0:1, :], ones[:, 0:1], xng[:, dc * 512:(dc + 1) * 512],
                    start=(dc == 0), stop=(dc == DC - 1)),
                    reads=rxng + [R["ones"]], writes=[bank[0]])
                p.op("tensor", lambda E, dc=dc: E.matmul(
                    pb[1][0:1, :], ones[:, 0:1], xng[:, 8192 + dc * 512: 8192 + (dc + 1) * 512],
                    start=(dc == 0), stop=(dc == DC - 1)),
                    reads=rxng + [R["ones"]], writes=[bank[1]],
                    inc=("SP" if dc == DC - 1 else None))
            p.op("vector", lambda E: E.tensor_copy(csrow[0:1, 0:512], pb[0][0:1, :]),
                 reads=[bank[0]], writes=[R["rowf"]])
            p.op("vector", lambda E: E.tensor_copy(csrow[0:1, 512:1024], pb[1][0:1, :]),
                 reads=[bank[1]], writes=[R["rowf"]])
            lo_tmp = rowb[0:1, 1024:2048]
            p.op("vector", lambda E: E.tensor_copy(cs2[0:1, :], csrow),
                 reads=[R["rowf"]], writes=[R["rowb"]])
            p.op("vector", lambda E, lo_tmp=lo_tmp: E.tensor_tensor(
                lo_tmp, csrow, cs2[0:1, :], ALU.subtract),
                reads=[R["rowf"]], writes=[R["rowb"]])
            p.dma("sync", "SRW", rowb[1:2, 0:1024], lo_tmp,
                  writes=[R["rowb"]], reads=[R["rowb"]])

            for hc in range(HC):
                hp = hc % 2
                gb = (bank[0], bank[1]) if hp == 0 else (bank[4], bank[5])
                ub = (bank[2], bank[3]) if hp == 0 else (bank[6], bank[7])
                gp = (pb[0], pb[1]) if hp == 0 else (pb[4], pb[5])
                up = (pb[2], pb[3]) if hp == 0 else (pb[6], pb[7])
                rwg, rwu = rf[f"wg{hp}"], rf[f"wu{hp}"]
                wgb = wgu[:, hp * 2048:(hp + 1) * 2048]
                wub = wgu[:, (2 + hp) * 2048:(3 + hp) * 2048]
                p.dma("gpsimd", f"GG{hp}", wgb, wg[hc], writes=[rwg])
                p.dma("gpsimd", f"GU{hp}", wub, wu[hc], writes=[rwu])
                for half in range(2):
                    p.op("tensor", lambda E, gp=gp, half=half: E.matmul(
                        gp[half][:, :], m128[:, :], cs2[:, half * 512:(half + 1) * 512],
                        start=True, stop=False),
                        reads=[R["rowb"], R["m128"]], writes=[gb[half]])
                    p.op("tensor", lambda E, up=up, half=half: E.matmul(
                        up[half][:, :], m128[:, :], cs2[:, half * 512:(half + 1) * 512],
                        start=True, stop=False),
                        reads=[R["rowb"], R["m128"]], writes=[ub[half]])
                for dc in range(DC):
                    glhs = wgb[:, dc * 128:(dc + 1) * 128]
                    ulhs = wub[:, dc * 128:(dc + 1) * 128]
                    for half in range(2):
                        rhs = xng[:, half * 8192 + dc * 512: half * 8192 + (dc + 1) * 512]
                        p.op("tensor", lambda E, gp=gp, half=half, glhs=glhs, rhs=rhs, dc=dc: E.matmul(
                            gp[half][:, :], glhs, rhs, start=False, stop=(dc == DC - 1)),
                            reads=[rwg] + rxng, writes=[gb[half]])
                        p.op("tensor", lambda E, up=up, half=half, ulhs=ulhs, rhs=rhs, dc=dc: E.matmul(
                            up[half][:, :], ulhs, rhs, start=False, stop=(dc == DC - 1)),
                            reads=[rwu] + rxng, writes=[ub[half]],
                            inc=("SP" if (dc == DC - 1 and half == 1) else None))
                for half in range(2):
                    p.op("scalar", lambda E, gp=gp, half=half, hc=hc: E.activation(
                        gsil[:, half * 512:(half + 1) * 512], gp[half][:, :], AF.Silu,
                        scale=sgb[:, hc:hc + 1]),
                        reads=[gb[half], R["sgb"]], writes=[rf["gsil"]])
                    p.op("scalar", lambda E, up=up, half=half, hc=hc: E.activation(
                        gup[:, half * 512:(half + 1) * 512], up[half][:, :], AF.Copy,
                        scale=sub[:, hc:hc + 1]),
                        reads=[ub[half], R["sub"]], writes=[rf["gup"]])
                p.op("vector", lambda E, hc=hc: E.tensor_tensor(
                    hidden[:, hc * GT:(hc + 1) * GT], gsil[:, :], gup[:, :], ALU.mult),
                    reads=[rf["gsil"], rf["gup"]], writes=[rhid])

            xtq = warf[:, 0:2048]
            oevf = war[:, 8192:12288].bitcast(F32)
            wdslot = [war[:, 4096:6144], war[:, 6144:8192], mrsave[:, 0:2048]]
            rd = {"wo0": rf["wg0"], "wo1": rf["wg1"],
                  "wd0": rf["wu0"], "wd1": rf["wu1"], "wd2": p.res("wd2")}

            WOSG = [(0, 4), (4, 8), (8, 12), (12, 16)]
            WDSG = [(i, min(i + 4, HC)) for i in range(0, HC, 4)]
            jobs = {qq: [] for qq in range(4)}
            wd_i = 0
            for qq in range(4):
                sgidx = 0
                for (kind, SGS, rres) in (("wo", WOSG, ("wo0", "wo1")),
                                          ("wd", WDSG, ("wd0", "wd1", "wd2"))):
                    for (a, b) in SGS:
                        wpar = (sgidx % 2) if kind == "wo" else (wd_i % 3)
                        if kind == "wd":
                            wd_i += 1
                        nblk = b - a
                        if kind == "wo":
                            dst = war[:, wpar * 2048: wpar * 2048 + nblk * 512]
                        else:
                            dst = wdslot[wpar][:, 0:nblk * 512]
                        jobs[qq].append(dict(qq=qq, kind=kind, a=a, nblk=nblk,
                                             dst=dst, rw=rd[rres[wpar]],
                                             dsem=(f"GO{wpar}" if kind == "wo" else f"GD{wpar}"),
                                             sgidx=sgidx, loaded=False))
                        sgidx += 1

            def emit_load(j):
                if j["loaded"]:
                    return
                j["loaded"] = True
                dst, nblk, rw, qq = j["dst"], j["nblk"], j["rw"], j["qq"]
                p.dma("gpsimd", j["dsem"], dst[:, 0:nblk * 512],
                      (wo if j["kind"] == "wo" else wd)[qq][:, j["a"] * 512:(j["a"] + nblk) * 512],
                      writes=[rw])
                if j["kind"] == "wd":
                    p.op("vector", lambda E, dst=dst, nblk=nblk, qq=qq: E.scalar_tensor_tensor(
                        dst.rearrange("p (k f) -> p k f", k=nblk),
                        dst.rearrange("p (k f) -> p k f", k=nblk), -128.0,
                        sdbc[:, qq * 512:(qq + 1) * 512].rearrange(
                            "p (r f) -> p r f", r=1).broadcast_to([128, nblk, 512]),
                        ALU.add, ALU.mult),
                        reads=[rw, rsdbc], writes=[rw])

            emit_load(jobs[0][6])
            emit_load(jobs[0][0])
            emit_load(jobs[0][4])
            emit_load(jobs[0][5])
            emit_load(jobs[0][1])
            p.barrier()
            if g == 0:
                xnres[1] = [p.res("xng1h0"), p.res("xng1h1")]

                def emit_xn_dma(i):
                    h2, qx = i // 2, i % 2
                    p.dma("gpsimd", f"GA{h2}",
                          xng[:, h2 * 8192 + qx * 4096: h2 * 8192 + (qx + 1) * 4096],
                          xblk[2 + h2][:, qx * 4096:(qx + 1) * 4096],
                          writes=[xnres[1][h2]])

            for qq in range(4):
                for ts in range(4):
                    p.dma("scalar", f"SX{ts}", xtq[:, ts * 512:(ts + 1) * 512],
                          xtok[gt0 + ts * 128: gt0 + (ts + 1) * 128,
                               qq * 512:(qq + 1) * 512], writes=[RD[f"xt{ts}"]])
                for ji, j in enumerate(jobs[qq]):
                    if g == 0 and qq == 0 and ji in (3, 5, 7, 9):
                        emit_xn_dma((ji - 3) // 2)
                    emit_load(j)
                    dst, nblk, rw = j["dst"], j["nblk"], j["rw"]
                    src_t = attn if j["kind"] == "wo" else hidden
                    rsrc = rattn if j["kind"] == "wo" else rhid
                    for ts in range(8):
                        for k in range(nblk):
                            chunk = j["a"] + k
                            laststop = (j["kind"] == "wd" and chunk == HC - 1)
                            do_inc = laststop or (k == nblk - 1 and ts == 7)
                            p.op("tensor", lambda E, ts=ts, chunk=chunk, k=k, src_t=src_t,
                                 dst=dst, st=(j["sgidx"] == 0 and k == 0), sp=laststop: E.matmul(
                                     pb[ts][:, :],
                                     src_t[:, chunk * GT + ts * 128: chunk * GT + (ts + 1) * 128],
                                     dst[:, k * 512:(k + 1) * 512],
                                     start=st, stop=sp),
                                 reads=[rw, rsrc], writes=[bank[ts]],
                                 inc=("SP" if do_inc else None))
                if qq + 1 < 4:
                    emit_load(jobs[qq + 1][0])
                    emit_load(jobs[qq + 1][4])
                    emit_load(jobs[qq + 1][5])
                for ts in range(8):
                    tpar = ts % 4
                    rxt, rov = RD[f"xt{tpar}"], RD[f"oev{tpar}"]
                    xt_ap = xtq[:, tpar * 512:(tpar + 1) * 512]
                    ov_ap = oevf[:, tpar * 512:(tpar + 1) * 512]
                    rows = slice(gt0 + ts * 128, gt0 + (ts + 1) * 128)
                    cols = slice(qq * 512, (qq + 1) * 512)
                    p.op("vector", lambda E, ts=ts, xt_ap=xt_ap, ov_ap=ov_ap: E.tensor_tensor(
                        ov_ap, pb[ts][:, :], xt_ap, ALU.add),
                        reads=[bank[ts], rxt], writes=[rov])
                    p.dma("sync", f"SY{tpar}", y[rows, cols], ov_ap, reads=[rov])
                    if ts < 4:
                        p.dma("scalar", f"SX{tpar}", xt_ap,
                              xtok[gt0 + (ts + 4) * 128: gt0 + (ts + 5) * 128, cols],
                              writes=[rxt])
                if g == 0:
                    emit_xn_build(1, dcs=range(qq * 4, qq * 4 + 4))
            p.barrier()

        fin = [(s, p.cnt[s]) for s in DMASEMS if p.cnt[s] > 0]
        p.op("sync", lambda E: E.nop(), extra=fin)

        with nc.Block() as block:
            @block.sync
            def _(E):
                for t in p.th["sync"]:
                    t(E, sems)

            @block.gpsimd
            def _(E):
                for t in p.th["gpsimd"]:
                    t(E, sems)

            @block.tensor
            def _(E):
                for t in p.th["tensor"]:
                    t(E, sems)

            @block.vector
            def _(E):
                for t in p.th["vector"]:
                    t(E, sems)

            @block.scalar
            def _(E):
                for t in p.th["scalar"]:
                    t(E, sems)

    return nc


_NC_CACHE = []


def _get_nc():
    if not _NC_CACHE:
        _NC_CACHE.append(build_nc())
    return _NC_CACHE[0]


def _prep_weights(L0_attn_Wq, L0_attn_bq, L0_attn_Wk, L0_attn_bk, L0_attn_Wv,
                  L0_attn_bv, L0_attn_Wo, L0_ffn_wq_gate, L0_ffn_s_gate,
                  L0_ffn_wq_up, L0_ffn_s_up, L0_ffn_wq_down, L0_ffn_s_down):
    C = np.ascontiguousarray
    Wq = np.asarray(L0_attn_Wq, dtype=np.float32)
    Wk = np.asarray(L0_attn_Wk, dtype=np.float32)
    Wv = np.asarray(L0_attn_Wv, dtype=np.float32)
    Wo = np.asarray(L0_attn_Wo, dtype=np.float32)

    def u8view(a):
        a = np.ascontiguousarray(np.asarray(a))
        assert a.dtype == np.int32
        return C(a.view(np.uint8).reshape(*a.shape, 4)[..., 0])

    wg8 = u8view(L0_ffn_wq_gate)
    wu8 = u8view(L0_ffn_wq_up)
    wd8 = u8view(L0_ffn_wq_down)

    d = {}
    d["wq"] = C(Wq.reshape(H, 128, DC, 128).transpose(0, 3, 2, 1).reshape(H, 128, D))
    Wkv = np.concatenate([Wk, Wv], axis=0)
    d["wkv"] = C(Wkv.reshape(1024, DC, 128).transpose(2, 1, 0))
    d["bkv"] = C(np.concatenate([np.asarray(L0_attn_bk, np.float32),
                                 np.asarray(L0_attn_bv, np.float32)])[None, :])
    d["bq"] = C(np.asarray(L0_attn_bq, np.float32).reshape(H, 128).T)
    d["wg"] = C(wg8.reshape(HC, 128, DC, 128).transpose(0, 3, 2, 1).reshape(HC, 128, D))
    d["wu"] = C(wu8.reshape(HC, 128, DC, 128).transpose(0, 3, 2, 1).reshape(HC, 128, D))
    d["sgi"] = C(np.asarray(L0_ffn_s_gate, np.float32).reshape(HC, 128).T)
    d["sui"] = C(np.asarray(L0_ffn_s_up, np.float32).reshape(HC, 128).T)
    d["wo"] = C(Wo.reshape(4, 512, H, 128).transpose(0, 3, 2, 1).reshape(4, 128, H * 512))
    d["wd"] = C(wd8.reshape(4, 512, HC, 128).transpose(0, 3, 2, 1).reshape(4, 128, HC * 512))
    d["sd"] = C(np.asarray(L0_ffn_s_down, np.float32)[None, :])
    return d


def kernel(L0_attn_hidden_states, L0_attn_Wq, L0_attn_bq, L0_attn_Wk, L0_attn_bk,
           L0_attn_Wv, L0_attn_bv, L0_attn_Wo, L0_ffn_wq_gate, L0_ffn_s_gate,
           L0_ffn_wq_up, L0_ffn_s_up, L0_ffn_wq_down, L0_ffn_s_down):
    x = np.asarray(L0_attn_hidden_states, dtype=np.float32)
    wdict = _prep_weights(L0_attn_Wq, L0_attn_bq, L0_attn_Wk, L0_attn_bk,
                          L0_attn_Wv, L0_attn_bv, L0_attn_Wo, L0_ffn_wq_gate,
                          L0_ffn_s_gate, L0_ffn_wq_up, L0_ffn_s_up,
                          L0_ffn_wq_down, L0_ffn_s_down)
    C = np.ascontiguousarray
    in_maps = []
    for c in range(8):
        b, half = c // 2, c % 2
        own = x[b][half * 2048:(half + 1) * 2048]
        m = dict(wdict)
        m["xblk"] = C(own.reshape(4, 512, DC, 128).transpose(0, 3, 2, 1).reshape(4, 128, 8192))
        m["xtok"] = C(own)
        in_maps.append(m)

    nc = _get_nc()
    res = run_bass_kernel_spmd(nc, in_maps, core_ids=list(range(8)))
    out = np.empty((4, 4096, 2048), dtype=np.float32)
    for c in range(8):
        b, half = c // 2, c % 2
        out[b, half * 2048:(half + 1) * 2048] = res.results[c]["y"]
    return out


if __name__ == "__main__":
    import reference as _r
    ins = _r.setup_inputs()
    got = kernel(**{k: np.asarray(v) for k, v in ins.items()})
    want = np.asarray(_r.reference(**ins))
    num = np.linalg.norm((got - want).ravel())
    den = np.linalg.norm(want.ravel())
    print("rel_err:", num / den)


# version 19
# speedup vs baseline: 1.0806x; 1.0806x over previous
import numpy as np
import ml_dtypes
import concourse.bass as bass
import concourse.mybir as mybir
from concourse.bass_utils import run_bass_kernel_spmd
from contextlib import ExitStack


def _install_neff_cache():
    try:
        import os as _os, hashlib as _hl, pathlib as _pl
        import concourse.bass2jax as _b2j
        import concourse.bass_utils as _bu
        if getattr(_bu, "_neff_cache_installed", False):
            return
        _orig = _bu.compile_bir_kernel
        _cdir = _pl.Path(_os.path.expanduser("~")) / ".bass_neff_cache"

        def _cached(bir_json, tmpdir, neff_name="file.neff"):
            cpath = None
            try:
                flags = _os.environ.get("AXON_NCC_FLAGS", "")
                key = _hl.sha256(
                    bytes(bir_json) + b"|" + flags.encode() + b"|" + neff_name.encode()
                ).hexdigest()
                cpath = _cdir / (key + ".neff")
                if cpath.is_file():
                    out = _os.path.join(tmpdir, neff_name)
                    with open(cpath, "rb") as f, open(out, "wb") as g:
                        g.write(f.read())
                    return out
            except Exception:
                cpath = None
            neff = _orig(bir_json, tmpdir, neff_name=neff_name)
            try:
                if cpath is not None:
                    _cdir.mkdir(parents=True, exist_ok=True)
                    tmp = str(cpath) + f".tmp{_os.getpid()}"
                    with open(neff, "rb") as f, open(tmp, "wb") as g:
                        g.write(f.read())
                    _os.replace(tmp, str(cpath))
            except Exception:
                pass
            return neff

        _bu.compile_bir_kernel = _cached
        _b2j.compile_bir_kernel = _cached
        _bu._neff_cache_installed = True
    except Exception:
        pass


_install_neff_cache()

F32 = mybir.dt.float32
BF16 = mybir.dt.bfloat16
U8 = mybir.dt.uint8
AF = mybir.ActivationFunctionType
ALU = mybir.AluOpType

D = 2048
S = 4096
H = 16
KVH = 4
HID = 5632
DC = D // 128
HC = HID // 128
TOWN = 2048
NG = 2
GT = 1024
NB0 = 8
B0T = 512
NOWN0 = 4
LN_EPS = 1e-5
ATT_EPS = 1e-6

ENGS = ("sync", "gpsimd", "tensor", "vector", "scalar")
ENGSEM = {"tensor": "SP", "vector": "SV", "scalar": "SA"}
DMASEMS = ("SSU", "SX0", "SX1", "SX2", "SX3", "SY0", "SY1", "SY2", "SY3", "SRW",
           "GX0", "GX1", "GW", "GA0", "GA1", "GQ0", "GQ1", "GB0", "GB1", "GB2", "GB3",
           "GG0", "GG1", "GU0", "GU1", "GO0", "GO1", "GD0", "GD1", "GD2", "GR", "GC", "GK")
ALLSEMS = ("SP", "SV", "SA") + DMASEMS


class Res:
    __slots__ = ("name", "ready", "readers", "pending_on")

    def __init__(self, name):
        self.name = name
        self.ready = []
        self.readers = []
        self.pending_on = set()


class Plan:
    def __init__(self):
        self.th = {e: [] for e in ENGS}
        self.cnt = {s: 0 for s in ALLSEMS}
        self.mark = {e: {s: 0 for s in ALLSEMS} for e in ENGS}
        self.pending = {e: [] for e in ENGS}
        self.barrier_evs = []
        self.last_ev = {}

    def op(self, eng, build, reads=(), writes=(), inc=None, inc_by=1, extra=()):
        if inc is None and eng in ("vector", "scalar"):
            inc = ENGSEM[eng]
        waits = list(extra)
        for r in reads:
            waits += r.ready
        for r in writes:
            for other in r.pending_on:
                if other != eng:
                    raise RuntimeError(f"write to {r.name} on {eng} while touches pending on {other}")
            waits += r.ready + r.readers
        fw = []
        for (s, v) in waits:
            if eng == "tensor" and s == "SP":
                continue
            if v > self.mark[eng][s]:
                self.mark[eng][s] = v
                fw.append((s, v))
        ev = None
        if inc is not None:
            self.cnt[inc] += inc_by
            ev = (inc, self.cnt[inc])

        def thunk(E, sems, fw=fw, ev=ev, build=build, inc_by=inc_by):
            for (s, v) in fw:
                E.wait_ge(sems[s], v)
            ins = build(E)
            if ev is not None:
                ins.then_inc(sems[ev[0]], inc_by)
        self.th[eng].append(thunk)
        for r in reads:
            self.pending[eng].append((r, "r"))
            r.pending_on.add(eng)
        for r in writes:
            self.pending[eng].append((r, "w"))
            r.pending_on.add(eng)
        if ev is not None:
            self.last_ev[eng] = ev
            for (r, kind) in self.pending[eng]:
                if kind == "w":
                    r.ready = [ev]
                    r.readers = []
                else:
                    r.readers.append(ev)
                r.pending_on.discard(eng)
            self.pending[eng] = []
        return ev

    def res(self, name):
        r = Res(name)
        r.ready = [(s, v) for (s, v) in self.barrier_evs
                   if not (s.startswith("SX") or s.startswith("SY"))]
        return r

    def dma(self, eng, sem, out_ap, in_ap, writes=(), reads=()):
        extra = [(s, v) for (s, v) in self.barrier_evs
                 if not (s.startswith("SX") or s.startswith("SY"))]
        return self.op(eng, lambda E: E.dma_start(out_ap, in_ap), reads=reads,
                       writes=writes, inc=sem, inc_by=16, extra=extra)

    def barrier(self):
        assert not any(self.pending[e] for e in ("tensor", "vector", "scalar")), \
            "barrier with unresolved pending ops"
        self.barrier_evs = [self.last_ev[e] for e in ("tensor", "vector", "scalar")
                            if e in self.last_ev]
        self.barrier_evs += [(s, self.cnt[s]) for s in DMASEMS if self.cnt[s] > 0]
        return self.barrier_evs


def build_nc():
    nc = bass.Bass("TRN2", target_bir_lowering=False)
    p = Plan()

    xblk = nc.dram_tensor("xblk", [NOWN0, 128, 8192], F32, kind="ExternalInput").ap()
    xtok = nc.dram_tensor("xtok", [TOWN, D], F32, kind="ExternalInput").ap()
    wq = nc.dram_tensor("wq", [H, 128, D], F32, kind="ExternalInput").ap()
    wkv = nc.dram_tensor("wkv", [128, DC, 1024], F32, kind="ExternalInput").ap()
    bkv = nc.dram_tensor("bkv", [1, 1024], F32, kind="ExternalInput").ap()
    bq = nc.dram_tensor("bq", [128, H], F32, kind="ExternalInput").ap()
    wg = nc.dram_tensor("wg", [HC, 128, D], U8, kind="ExternalInput").ap()
    wu = nc.dram_tensor("wu", [HC, 128, D], U8, kind="ExternalInput").ap()
    sgi = nc.dram_tensor("sgi", [128, HC], F32, kind="ExternalInput").ap()
    sui = nc.dram_tensor("sui", [128, HC], F32, kind="ExternalInput").ap()
    wo = nc.dram_tensor("wo", [4, 128, H * 512], F32, kind="ExternalInput").ap()
    wd = nc.dram_tensor("wd", [4, 128, HC * 512], U8, kind="ExternalInput").ap()
    sd = nc.dram_tensor("sd", [1, D], F32, kind="ExternalInput").ap()
    y = nc.dram_tensor("y", [TOWN, D], F32, kind="ExternalOutput").ap()
    cc_in = nc.dram_tensor("cc_in", [128, 528], F32).ap()
    cc_out = nc.dram_tensor("cc_out", [128, 528], F32).ap()

    with ExitStack() as top:
        sbuf = lambda n, s, d: top.enter_context(nc.sbuf_tensor(n, s, d))
        sems = {n: top.enter_context(nc.semaphore(n)) for n in ALLSEMS}

        hidden = sbuf("hidden", [128, HC * GT], BF16)
        attn = sbuf("attn", [128, H * GT], BF16)
        xng = sbuf("xng", [128, DC * GT], BF16)
        war = sbuf("war", [128, 12288], BF16)
        warf = sbuf("warf", [128, 2048], F32)
        stat = sbuf("stat", [128, 2048], BF16)

        cst = sbuf("cst", [128, 8], F32)
        CVALS = [0.0, 1.0, -1.0, -0.5, LN_EPS, ATT_EPS]
        for i, v in enumerate(CVALS):
            nc.const_aps.register(F32, v, cst[:, i:i + 1])
        ones = sbuf("ones", [128, 128], BF16)
        m128 = sbuf("m128", [2, 128], BF16)
        kvb = sbuf("kvb", [128, KVH * 128], BF16)
        ksb = sbuf("ksb", [128, KVH], BF16)
        bqb = sbuf("bqb", [128, H], F32)
        bqn = sbuf("bqn", [128, H], F32)
        sgb = sbuf("sgb", [128, HC], F32)
        sub = sbuf("sub", [128, HC], F32)
        mrsave = sbuf("mrsave", [128, 4096], BF16)
        scr = sbuf("scr", [128, 512], BF16)
        rowf = sbuf("rowf", [1, 1024], F32)
        rowb = sbuf("rowb", [2, 2048], BF16)
        lnrow = rowf[0:1, 0:1024]
        csrow = rowf[0:1, 0:1024]
        cs2 = rowb[0:2, 0:1024]
        rec = rowb[0:1, 1024:2048]
        sdrow = rowb[0:1, 0:2048]

        pb = [top.enter_context(nc.psum_tensor(f"pb{i}", [128, 512], F32)) for i in range(8)]
        bank = [Res(f"bank{i}") for i in range(8)]

        R = {n: Res(n) for n in ("cst", "ones", "m128", "kvb", "ksb", "bqb", "bqn",
                                 "sgb", "sub", "mrsave", "scr", "rowf", "rowb")}
        RD = {n: Res(n) for n in ("xt0", "xt1", "xt2", "xt3",
                                  "oev0", "oev1", "oev2", "oev3")}

        p.dma("sync", "SSU", bqb[:, :], bq, writes=[R["bqb"]])
        p.dma("sync", "SSU", sgb[:, :], sgi, writes=[R["sgb"]])
        p.dma("sync", "SSU", sub[:, :], sui, writes=[R["sub"]])
        for _r in (R["bqb"], R["sgb"], R["sub"]):
            _r.ready = [("SSU", p.cnt["SSU"])]
        for i, v in enumerate(CVALS):
            p.op("vector", lambda E, i=i, v=v: E.memset(cst[:, i:i + 1], v), writes=[R["cst"]])
        p.op("vector", lambda E: E.memset(ones[:, :], 1.0), writes=[R["ones"]])
        p.op("vector", lambda E: E.memset(m128[:, :], -128.0), writes=[R["m128"]])
        p.op("vector", lambda E: E.tensor_scalar_mul(bqn[:, :], bqb[:, :], -1.0),
             reads=[R["bqb"]], writes=[R["bqn"]])
        consts_ev = [("SV", p.cnt["SV"])]
        p.dma("gpsimd", "GR", sdrow, sd, writes=[R["rowb"]])

        wkvb = hidden[:, 0:16384]
        xsq = hidden[:, 32768:40960]
        skb = hidden[:, 40960:41984]
        vab = hidden[:, 41984:43040]
        bkvb = hidden[0:1, 43072:44096]
        ftmp = warf[:, 0:1024]
        varb = warf[:, 1024:2048]
        r0 = {n: p.res(n) for n in ("wkvb", "xsq", "xnb0", "xnb1",
                                    "skb0", "skb1", "vab0", "vab1", "stat", "ftmp",
                                    "varb", "bkvb")}
        for par in range(2):
            for q in range(4):
                r0[f"xbf{par}q{q}"] = p.res(f"xbf{par}q{q}")

        for par in range(2):
            p.op("vector", lambda E, par=par: E.memset(
                vab.rearrange("p (q g c) -> p q g c", q=2, g=4)[:, par, :, 128:129], 1.0),
                writes=[r0[f"vab{par}"]])

        def emit_w0_loads():
            p.dma("gpsimd", "GK", bkvb, bkv, writes=[r0["bkvb"]])
            wkv_evs = []
            for i, sem in enumerate(("GW", "GG0", "GU0", "GG1")):
                rq = Res(f"wkvq{i}")
                p.dma("gpsimd", sem,
                      wkvb.rearrange("p (c f) -> p c f", c=DC)[:, 4 * i:4 * i + 4, :],
                      wkv[:, 4 * i:4 * i + 4, :], writes=[rq])
                wkv_evs += rq.ready
            r0["wkvb"].ready = wkv_evs
            for h in range(2):
                p.dma("gpsimd", f"GQ{h}", war[:, h * 2048:(h + 1) * 2048], wq[h],
                      writes=[wqres0[h]])
        wqres0 = [p.res(f"g0wq{i}") for i in range(4)]

        xnres = {0: [p.res("xng0h0"), p.res("xng0h1")]}

        def emit_xn_build(g, dcs=range(DC)):
            if g not in xnres:
                xnres[g] = [p.res(f"xng{g}h0"), p.res(f"xng{g}h1")]
                for h2 in range(2):
                    p.dma("gpsimd", f"GA{h2}",
                          xng[:, h2 * 8192:(h2 + 1) * 8192],
                          xblk[g * 2 + h2], writes=[xnres[g][h2]])
            for h2 in range(2):
                rx = xnres[g][h2]
                blk = (g * GT + h2 * 512) // 512
                m_ap = mrsave[:, blk * 1024: blk * 1024 + 512]
                r_ap = mrsave[:, blk * 1024 + 512: blk * 1024 + 1024]
                for dc in dcs:
                    sl = xng[:, h2 * 8192 + dc * 512: h2 * 8192 + (dc + 1) * 512]
                    p.op("vector", lambda E, sl=sl, m_ap=m_ap: E.tensor_tensor(
                        sl, sl, m_ap, ALU.subtract),
                        reads=[rx, R["mrsave"]], writes=[rx])
                    p.op("vector", lambda E, sl=sl, r_ap=r_ap: E.tensor_tensor(
                        sl, sl, r_ap, ALU.mult),
                        reads=[R["mrsave"]], writes=[rx])

        st_idx = 0
        pend_kv = []

        def flush_kv():
            while pend_kv:
                sk_ap, va_base, rsk, rva, sti = pend_kv.pop(0)
                for g4 in range(4):
                    bnk = bank[6 + g4 // 2]
                    pp = pb[6 + g4 // 2]
                    col = (g4 % 2) * 256
                    p.op("tensor", lambda E, g4=g4, sk_ap=sk_ap, va_base=va_base, pp=pp, col=col,
                         first=(sti == 0 and g4 % 2 == 0), last=(sti == 31): E.matmul(
                             pp[:, col:col + 129], sk_ap[:, g4 * 128:(g4 + 1) * 128],
                             vab[:, va_base + g4 * 132: va_base + g4 * 132 + 129],
                             start=first, stop=last, skip_group_check=True),
                        reads=[rsk, rva], writes=[bnk],
                        inc=("SP" if g4 == 3 else None))

        def emit_xdma(b):
            par = b % 2
            xb = hidden[:, 16384 + par * 8192: 16384 + (par + 1) * 8192]
            for q in range(4):
                p.dma("gpsimd", (f"GX{par}", f"GA{par}", f"GB{par}", f"GB{par + 2}")[q],
                      xb[:, q * 2048:(q + 1) * 2048],
                      xblk[b][:, q * 2048:(q + 1) * 2048],
                      writes=[r0[f"xbf{par}q{q}"]])

        def emit_square(b):
            par = b % 2
            xb = hidden[:, 16384 + par * 8192: 16384 + (par + 1) * 8192]
            for q in range(4):
                p.op("vector", lambda E, xb=xb, q=q: E.tensor_tensor(
                    xsq[:, q * 2048:(q + 1) * 2048], xb[:, q * 2048:(q + 1) * 2048],
                    xb[:, q * 2048:(q + 1) * 2048], ALU.mult),
                    reads=[r0[f"xbf{par}q{q}"]], writes=[r0["xsq"]], extra=consts_ev)

        def stat_aps(b):
            return (mrsave[:, b * 1024: b * 1024 + 512],
                    mrsave[:, b * 1024 + 512: b * 1024 + 1024], R["mrsave"])

        def emit_s1(b):
            par = b % 2
            xb = hidden[:, 16384 + par * 8192: 16384 + (par + 1) * 8192]
            for dc in range(DC):
                p.op("tensor", lambda E, dc=dc, xb=xb: E.matmul(
                    pb[4][:, :], ones[:, :], xb[:, dc * 512:(dc + 1) * 512],
                    start=(dc == 0), stop=(dc == DC - 1)),
                    reads=[r0[f"xbf{par}q{dc // 4}"], R["ones"]], writes=[bank[4]])

        def emit_s2(b):
            for dc in range(DC):
                p.op("tensor", lambda E, dc=dc: E.matmul(
                    pb[5][:, :], ones[:, :], xsq[:, dc * 512:(dc + 1) * 512],
                    start=(dc == 0), stop=(dc == DC - 1)),
                    reads=[r0["xsq"], R["ones"]], writes=[bank[5]],
                    inc=("SP" if dc == DC - 1 else None))

        def emit_mr_xn(b):
            par = b % 2
            xb = hidden[:, 16384 + par * 8192: 16384 + (par + 1) * 8192]
            if b < 2:
                xn, rxnw = xng[:, b * 8192:(b + 1) * 8192], xnres[0][b]
            else:
                xn, rxnw = attn[:, par * 8192:(par + 1) * 8192], r0[f"xnb{par}"]
            m_ap, r_ap, rmr = stat_aps(b)
            p.op("vector", lambda E, m_ap=m_ap: E.tensor_scalar_mul(m_ap, pb[4][:, :], 1.0 / 2048.0),
                 reads=[bank[4]], writes=[rmr])
            p.op("scalar", lambda E, m_ap=m_ap: E.activation(varb[:, 0:512], m_ap, AF.Square),
                 reads=[rmr], writes=[r0["varb"]])
            p.op("vector", lambda E: E.scalar_tensor_tensor(
                varb[:, 512:1024], pb[5][:, :], 1.0 / 2048.0, varb[:, 0:512], ALU.mult, ALU.subtract),
                reads=[bank[5], r0["varb"]], writes=[r0["varb"]])
            p.op("scalar", lambda E: E.activation(varb[:, 0:512], varb[:, 512:1024], AF.Ln, bias=LN_EPS),
                 reads=[r0["varb"]], writes=[r0["varb"]])
            p.op("scalar", lambda E, r_ap=r_ap: E.activation(r_ap, varb[:, 0:512], AF.Exp, scale=-0.5),
                 reads=[r0["varb"]], writes=[rmr])
            for dc in range(DC):
                p.op("vector", lambda E, dc=dc, xb=xb, m_ap=m_ap: E.tensor_tensor(
                    scr[:, :], xb[:, dc * 512:(dc + 1) * 512], m_ap, ALU.subtract),
                    reads=[r0[f"xbf{par}q{dc // 4}"], rmr], writes=[R["scr"]])
                p.op("vector", lambda E, dc=dc, xn=xn, r_ap=r_ap: E.tensor_tensor(
                    xn[:, dc * 512:(dc + 1) * 512], scr[:, :], r_ap, ALU.mult),
                    reads=[R["scr"], rmr], writes=[rxnw])

        NOWN = 4
        emit_xdma(0)
        emit_xdma(1)
        emit_w0_loads()
        emit_square(0)
        emit_s1(0)
        emit_s2(0)
        emit_mr_xn(0)
        emit_square(1)

        for b in range(NOWN):
            par = b % 2
            xb = hidden[:, 16384 + par * 8192: 16384 + (par + 1) * 8192]
            if b < 2:
                xn, rxn = xng[:, b * 8192:(b + 1) * 8192], xnres[0][b]
            else:
                xn, rxn = attn[:, par * 8192:(par + 1) * 8192], r0[f"xnb{par}"]
            if b + 2 < NOWN:
                emit_xdma(b + 2)
            if b + 1 < NOWN:
                emit_s1(b + 1)
            for s in range(4):
                ps = s % 2
                bk_, bv_ = (bank[0], bank[1]) if ps == 0 else (bank[2], bank[3])
                pk, pv = (pb[0], pb[1]) if ps == 0 else (pb[2], pb[3])
                rsk, rva = r0[f"skb{ps}"], r0[f"vab{ps}"]
                sk_ap = skb[:, ps * 512:(ps + 1) * 512]
                va_base = ps * 528
                p.op("tensor", lambda E, pk=pk: E.matmul(
                    pk[:, :], ones[0:1, :], bkvb[0:1, 0:512], start=True, stop=False),
                    reads=[r0["bkvb"], R["ones"]], writes=[bk_])
                p.op("tensor", lambda E, pv=pv: E.matmul(
                    pv[:, :], ones[0:1, :], bkvb[0:1, 512:1024], start=True, stop=False),
                    reads=[r0["bkvb"], R["ones"]], writes=[bv_])
                for dc in range(DC):
                    lhs = xn[:, dc * 512 + s * 128: dc * 512 + (s + 1) * 128]
                    p.op("tensor", lambda E, pk=pk, dc=dc, lhs=lhs: E.matmul(
                        pk[:, :], lhs, wkvb[:, dc * 1024: dc * 1024 + 512],
                        start=False, stop=(dc == DC - 1)),
                        reads=[rxn, r0["wkvb"]], writes=[bk_])
                    p.op("tensor", lambda E, pv=pv, dc=dc, lhs=lhs: E.matmul(
                        pv[:, :], lhs, wkvb[:, dc * 1024 + 512: (dc + 1) * 1024],
                        start=False, stop=(dc == DC - 1)),
                        reads=[rxn, r0["wkvb"]], writes=[bv_],
                        inc=("SP" if dc == DC - 1 else None))
                if s == 0 and b + 1 < NOWN:
                    emit_s2(b + 1)
                    emit_mr_xn(b + 1)
                flush_kv()
                p.op("scalar", lambda E, pk=pk: E.activation(varb[:, 0:512], pk[:, :], AF.Relu),
                     reads=[bk_], writes=[r0["varb"]])
                p.op("scalar", lambda E, pk=pk: E.activation(ftmp[:, 0:512], pk[:, :], AF.Relu, scale=-1.0),
                     reads=[bk_], writes=[r0["ftmp"]])
                p.op("scalar", lambda E: E.activation(ftmp[:, 512:1024], ftmp[:, 0:512], AF.Exp, scale=-1.0),
                     reads=[r0["ftmp"]], writes=[r0["ftmp"]])
                p.op("vector", lambda E, sk_ap=sk_ap: E.tensor_tensor(
                    sk_ap, varb[:, 0:512], ftmp[:, 512:1024], ALU.add),
                    reads=[r0["varb"], r0["ftmp"]], writes=[rsk])
                p.op("vector", lambda E, pv=pv, va_base=va_base: E.tensor_copy(
                    vab[:, va_base:va_base + 528].rearrange("p (g c) -> p g c", g=4)[:, :, 0:128],
                    pv[:, :].rearrange("p (g c) -> p g c", g=4)),
                    reads=[bv_], writes=[rva])
                pend_kv.append((sk_ap, va_base, rsk, rva, st_idx))
                st_idx += 1
                if s == 2 and b + 2 < NOWN:
                    emit_square(b + 2)
        flush_kv()
        kvp = warf[:, 0:528]
        rkvp = Res("kvp")
        p.op("vector", lambda E: E.memset(kvp[:, 0:528], 0.0),
             reads=[r0["ftmp"]], writes=[rkvp])
        for g4 in range(4):
            pp = pb[6 + g4 // 2]
            col = (g4 % 2) * 256
            p.op("vector", lambda E, pp=pp, col=col, g4=g4: E.tensor_copy(
                kvp[:, g4 * 132: g4 * 132 + 129], pp[:, col:col + 129]),
                reads=[bank[6 + g4 // 2]], writes=[rkvp])
        rcc_in = Res("cc_in")
        p.dma("sync", "SRW", cc_in, kvp[:, 0:528], reads=[rkvp], writes=[rcc_in])
        rcc = Res("cc")
        p.op("gpsimd", lambda E: E.collective_compute(
            "AllReduce", ALU.add, [[0, 1], [2, 3], [4, 5], [6, 7]],
            [cc_in.opt()], [cc_out.opt()]),
            reads=[rcc_in], writes=[rcc], inc="GC", inc_by=1)
        p.dma("sync", "SSU", kvp[:, 0:528], cc_out, writes=[rkvp], reads=[rcc])

        def emit_kv_unpack():
            for g4 in range(4):
                p.op("vector", lambda E, g4=g4: E.tensor_copy(
                    kvb[:, g4 * 128:(g4 + 1) * 128], kvp[:, g4 * 132: g4 * 132 + 128]),
                    reads=[rkvp], writes=[R["kvb"]])
                p.op("vector", lambda E, g4=g4: E.tensor_copy(
                    ksb[:, g4:g4 + 1], kvp[:, g4 * 132 + 128: g4 * 132 + 129]),
                    reads=[rkvp], writes=[R["ksb"]])

        sdbc = stat[:, 0:2048]
        rsdbc = Res("sdbc")
        for qq in range(4):
            p.op("tensor", lambda E, qq=qq: E.matmul(
                pb[4][:, :], ones[0:1, :], sdrow[0:1, qq * 512:(qq + 1) * 512]),
                reads=[R["rowb"], R["ones"]], writes=[bank[4]], inc="SP")
            p.op("vector", lambda E, qq=qq: E.tensor_copy(
                sdbc[:, qq * 512:(qq + 1) * 512], pb[4][:, :]),
                reads=[bank[4]], writes=[rsdbc])

        def join(*ress):
            evs = []
            for r in ress:
                evs += r.ready + r.readers
            return evs

        wq_next = {}
        for g in range(NG):
            gt0 = g * GT
            rxng = xnres[g]

            def atm_ap(w, pa):
                off = 16384 + pa * 2048 + w * 1024
                return hidden[:, off:off + 1024]
            if g == 0:
                def mkres(name, evs):
                    r = Res(name)
                    r.ready = list(evs)
                    return r
                sq_evs = join(r0["wkvb"])
                atm_evs = join(*[r0[f"xbf0q{q}"] for q in range(4)])
                rsq = [mkres(f"sq{h}", sq_evs) for h in range(H)]
                ratm = [mkres(f"atm{pa}", atm_evs) for pa in range(2)]
                rattn = mkres("attn", join(r0["xnb0"], r0["xnb1"]))
                rhid = mkres("hidden", [])
                rwq, npre = wqres0, 2
            else:
                rsq = [p.res(f"sq{h}") for h in range(H)]
                ratm = [p.res(f"atm{pa}") for pa in range(2)]
                rattn = p.res("attn")
                rhid = p.res("hidden")
                rwq, npre = wq_next["res"], 2

            def emit_q_half(h, half):
                hp = h % 2
                rw = rwq[h % 4]
                wqb = war[:, (h % 4) * 2048: (h % 4 + 1) * 2048]
                if half == 0 and h >= npre:
                    p.dma("gpsimd", f"GQ{hp}", wqb, wq[h], writes=[rw])
                b0, b1 = (bank[0], bank[1]) if hp == 0 else (bank[2], bank[3])
                q0, q1 = (pb[0], pb[1]) if hp == 0 else (pb[2], pb[3])
                sq = hidden[:, h * 1024:(h + 1) * 1024]
                atm1, atm2 = atm_ap(0, hp), atm_ap(1, hp)
                qq, bnk = (q0, b0) if half == 0 else (q1, b1)
                for dc in range(DC):
                    lhs = wqb[:, dc * 128:(dc + 1) * 128]
                    p.op("tensor", lambda E, qq=qq, dc=dc, lhs=lhs, half=half: E.matmul(
                        qq[:, :], lhs, xng[:, half * 8192 + dc * 512: half * 8192 + (dc + 1) * 512],
                        start=(dc == 0), stop=(dc == DC - 1)),
                        reads=[rw] + rxng, writes=[bnk],
                        inc=("SP" if dc == DC - 1 else None))
                hsl = slice(half * 512, (half + 1) * 512)
                p.op("scalar", lambda E, qq=qq, hsl=hsl, h=h, atm1=atm1: E.activation(
                    atm1[:, hsl], qq[:, :], AF.Relu, bias=bqb[:, h:h + 1]),
                    reads=[bnk, R["bqb"]], writes=[ratm[hp]])
                p.op("scalar", lambda E, qq=qq, hsl=hsl, h=h, atm2=atm2: E.activation(
                    atm2[:, hsl], qq[:, :], AF.Relu, scale=-1.0, bias=bqn[:, h:h + 1]),
                    reads=[bnk, R["bqn"]], writes=[ratm[hp]])
                p.op("scalar", lambda E, sq=sq, hsl=hsl, atm2=atm2: E.activation(
                    sq[:, hsl], atm2[:, hsl], AF.Exp, scale=-1.0),
                    reads=[ratm[hp]], writes=[rsq[h]])
                p.op("vector", lambda E, sq=sq, hsl=hsl, atm1=atm1: E.tensor_tensor(
                    sq[:, hsl], sq[:, hsl], atm1[:, hsl], ALU.add),
                    reads=[ratm[hp]], writes=[rsq[h]])

            def emit_qkv_half(h, half):
                hp = h % 2
                g4 = h // 4
                sq = hidden[:, h * 1024:(h + 1) * 1024]
                kb = (bank[4], bank[5]) if hp == 0 else (bank[6], bank[7])
                KP = (pb[4], pb[5]) if hp == 0 else (pb[6], pb[7])
                qb = (bank[0], bank[1]) if hp == 0 else (bank[2], bank[3])
                qp = (pb[0], pb[1]) if hp == 0 else (pb[2], pb[3])
                hsl = slice(half * 512, (half + 1) * 512)
                p.op("tensor", lambda E, K=KP[half], g4=g4, sq=sq, hsl=hsl: E.matmul(
                    K[:, :], kvb[:, g4 * 128:(g4 + 1) * 128], sq[:, hsl]),
                    reads=[R["kvb"], rsq[h]], writes=[kb[half]])
                p.op("tensor", lambda E, q=qp[half], g4=g4, sq=sq, hsl=hsl: E.matmul(
                    q[0:1, :], ksb[:, g4:g4 + 1], sq[:, hsl]),
                    reads=[R["ksb"], rsq[h]], writes=[qb[half]], inc="SP")
                p.op("scalar", lambda E, q=qp[half], hsl=hsl: E.activation(
                    lnrow[0:1, hsl], q[0:1, :], AF.Ln, bias=ATT_EPS),
                    reads=[qb[half]], writes=[R["rowf"]])
                if half == 1:
                    p.op("scalar", lambda E: E.activation(rec, lnrow, AF.Exp, scale=-1.0),
                         reads=[R["rowf"]], writes=[R["rowb"]])

            def emit_rb_attn(h):
                hp = h % 2
                b0, b1 = (bank[0], bank[1]) if hp == 0 else (bank[2], bank[3])
                q0, q1 = (pb[0], pb[1]) if hp == 0 else (pb[2], pb[3])
                k0, k1 = (bank[4], bank[5]) if hp == 0 else (bank[6], bank[7])
                K0, K1 = (pb[4], pb[5]) if hp == 0 else (pb[6], pb[7])
                sq = hidden[:, h * 1024:(h + 1) * 1024]
                p.op("tensor", lambda E, q0=q0: E.matmul(
                    q0[:, :], ones[0:1, :], rec[0:1, 0:512]),
                    reads=[R["rowb"], R["ones"]], writes=[b0])
                p.op("tensor", lambda E, q1=q1: E.matmul(
                    q1[:, :], ones[0:1, :], rec[0:1, 512:1024]),
                    reads=[R["rowb"], R["ones"]], writes=[b1], inc="SP")
                p.op("vector", lambda E, K0=K0, sq=sq: E.tensor_copy(sq[:, 0:512], K0[:, :]),
                     reads=[k0], writes=[rsq[h]])
                p.op("vector", lambda E, K1=K1, sq=sq: E.tensor_copy(sq[:, 512:1024], K1[:, :]),
                     reads=[k1], writes=[rsq[h]])
                p.op("vector", lambda E, q0=q0, sq=sq, h=h: E.tensor_tensor(
                    attn[:, h * GT: h * GT + 512], sq[:, 0:512], q0[:, :], ALU.mult),
                    reads=[rsq[h], b0], writes=[rattn])
                p.op("vector", lambda E, q1=q1, sq=sq, h=h: E.tensor_tensor(
                    attn[:, h * GT + 512: (h + 1) * GT], sq[:, 512:1024], q1[:, :], ALU.mult),
                    reads=[rsq[h], b1], writes=[rattn])

            def emit_colsum():
                for dc in range(DC):
                    p.op("tensor", lambda E, dc=dc: E.matmul(
                        pb[4][0:1, :], ones[:, 0:1], xng[:, dc * 512:(dc + 1) * 512],
                        start=(dc == 0), stop=(dc == DC - 1)),
                        reads=rxng + [R["ones"]], writes=[bank[4]])
                    p.op("tensor", lambda E, dc=dc: E.matmul(
                        pb[5][0:1, :], ones[:, 0:1], xng[:, 8192 + dc * 512: 8192 + (dc + 1) * 512],
                        start=(dc == 0), stop=(dc == DC - 1)),
                        reads=rxng + [R["ones"]], writes=[bank[5]],
                        inc=("SP" if dc == DC - 1 else None))
                p.op("vector", lambda E: E.tensor_copy(csrow[0:1, 0:512], pb[4][0:1, :]),
                     reads=[bank[4]], writes=[R["rowf"]])
                p.op("vector", lambda E: E.tensor_copy(csrow[0:1, 512:1024], pb[5][0:1, :]),
                     reads=[bank[5]], writes=[R["rowf"]])
                lo_tmp = rowb[0:1, 1024:2048]
                p.op("vector", lambda E: E.tensor_copy(cs2[0:1, :], csrow),
                     reads=[R["rowf"]], writes=[R["rowb"]])
                p.op("vector", lambda E, lo_tmp=lo_tmp: E.tensor_tensor(
                    lo_tmp, csrow, cs2[0:1, :], ALU.subtract),
                    reads=[R["rowf"]], writes=[R["rowb"]])
                p.dma("sync", "SRW", rowb[1:2, 0:1024], lo_tmp,
                      writes=[R["rowb"]], reads=[R["rowb"]])

            QKD = 7
            for step in range(H + QKD):
                if step == H - 1:
                    emit_colsum()
                if step >= QKD:
                    h2 = step - QKD
                    emit_qkv_half(h2, 0)
                    emit_qkv_half(h2, 1)
                    emit_rb_attn(h2)
                if step < H:
                    emit_q_half(step, 0)
                    emit_q_half(step, 1)
                if g == 0 and step == 5:
                    emit_kv_unpack()
            p.barrier()

            wgu = war[:, 0:8192]
            gsil = war[:, 8192:9216]
            gup = war[:, 9216:10240]
            rf = {n: p.res(n) for n in ("wg0", "wg1", "wu0", "wu1", "gsil", "gup")}

            for hc in range(HC):
                hp = hc % 2
                gb = (bank[0], bank[1]) if hp == 0 else (bank[4], bank[5])
                ub = (bank[2], bank[3]) if hp == 0 else (bank[6], bank[7])
                gp = (pb[0], pb[1]) if hp == 0 else (pb[4], pb[5])
                up = (pb[2], pb[3]) if hp == 0 else (pb[6], pb[7])
                rwg, rwu = rf[f"wg{hp}"], rf[f"wu{hp}"]
                wgb = wgu[:, hp * 2048:(hp + 1) * 2048]
                wub = wgu[:, (2 + hp) * 2048:(3 + hp) * 2048]
                p.dma("gpsimd", f"GG{hp}", wgb, wg[hc], writes=[rwg])
                p.dma("gpsimd", f"GU{hp}", wub, wu[hc], writes=[rwu])
                for half in range(2):
                    p.op("tensor", lambda E, gp=gp, half=half: E.matmul(
                        gp[half][:, :], m128[:, :], cs2[:, half * 512:(half + 1) * 512],
                        start=True, stop=False),
                        reads=[R["rowb"], R["m128"]], writes=[gb[half]])
                    p.op("tensor", lambda E, up=up, half=half: E.matmul(
                        up[half][:, :], m128[:, :], cs2[:, half * 512:(half + 1) * 512],
                        start=True, stop=False),
                        reads=[R["rowb"], R["m128"]], writes=[ub[half]])
                for dc in range(DC):
                    glhs = wgb[:, dc * 128:(dc + 1) * 128]
                    ulhs = wub[:, dc * 128:(dc + 1) * 128]
                    for half in range(2):
                        rhs = xng[:, half * 8192 + dc * 512: half * 8192 + (dc + 1) * 512]
                        p.op("tensor", lambda E, gp=gp, half=half, glhs=glhs, rhs=rhs, dc=dc: E.matmul(
                            gp[half][:, :], glhs, rhs, start=False, stop=(dc == DC - 1)),
                            reads=[rwg] + rxng, writes=[gb[half]])
                        p.op("tensor", lambda E, up=up, half=half, ulhs=ulhs, rhs=rhs, dc=dc: E.matmul(
                            up[half][:, :], ulhs, rhs, start=False, stop=(dc == DC - 1)),
                            reads=[rwu] + rxng, writes=[ub[half]],
                            inc=("SP" if (dc == DC - 1 and half == 1) else None))
                for half in range(2):
                    p.op("scalar", lambda E, gp=gp, half=half, hc=hc: E.activation(
                        gsil[:, half * 512:(half + 1) * 512], gp[half][:, :], AF.Silu,
                        scale=sgb[:, hc:hc + 1]),
                        reads=[gb[half], R["sgb"]], writes=[rf["gsil"]])
                    p.op("scalar", lambda E, up=up, half=half, hc=hc: E.activation(
                        gup[:, half * 512:(half + 1) * 512], up[half][:, :], AF.Copy,
                        scale=sub[:, hc:hc + 1]),
                        reads=[ub[half], R["sub"]], writes=[rf["gup"]])
                p.op("vector", lambda E, hc=hc: E.tensor_tensor(
                    hidden[:, hc * GT:(hc + 1) * GT], gsil[:, :], gup[:, :], ALU.mult),
                    reads=[rf["gsil"], rf["gup"]], writes=[rhid])

            xtq = warf[:, 0:2048]
            oevf = war[:, 8192:12288].bitcast(F32)
            wdslot = [war[:, 4096:6144], war[:, 6144:8192], mrsave[:, 0:2048]]
            rd = {"wo0": rf["wg0"], "wo1": rf["wg1"],
                  "wd0": rf["wu0"], "wd1": rf["wu1"], "wd2": p.res("wd2")}

            WOSG = [(0, 4), (4, 8), (8, 12), (12, 16)]
            WDSG = [(i, min(i + 4, HC)) for i in range(0, HC, 4)]
            jobs = {qq: [] for qq in range(4)}
            wd_i = 0
            for qq in range(4):
                sgidx = 0
                for (kind, SGS, rres) in (("wo", WOSG, ("wo0", "wo1")),
                                          ("wd", WDSG, ("wd0", "wd1", "wd2"))):
                    for (a, b) in SGS:
                        wpar = (sgidx % 2) if kind == "wo" else (wd_i % 3)
                        if kind == "wd":
                            wd_i += 1
                        nblk = b - a
                        if kind == "wo":
                            dst = war[:, wpar * 2048: wpar * 2048 + nblk * 512]
                        else:
                            dst = wdslot[wpar][:, 0:nblk * 512]
                        jobs[qq].append(dict(qq=qq, kind=kind, a=a, nblk=nblk,
                                             dst=dst, rw=rd[rres[wpar]],
                                             dsem=(f"GO{wpar}" if kind == "wo" else f"GD{wpar}"),
                                             sgidx=sgidx, loaded=False))
                        sgidx += 1

            def emit_load(j):
                if j["loaded"]:
                    return
                j["loaded"] = True
                dst, nblk, rw, qq = j["dst"], j["nblk"], j["rw"], j["qq"]
                p.dma("gpsimd", j["dsem"], dst[:, 0:nblk * 512],
                      (wo if j["kind"] == "wo" else wd)[qq][:, j["a"] * 512:(j["a"] + nblk) * 512],
                      writes=[rw])
                if j["kind"] == "wd":
                    p.op("vector", lambda E, dst=dst, nblk=nblk, qq=qq: E.scalar_tensor_tensor(
                        dst.rearrange("p (k f) -> p k f", k=nblk),
                        dst.rearrange("p (k f) -> p k f", k=nblk), -128.0,
                        sdbc[:, qq * 512:(qq + 1) * 512].rearrange(
                            "p (r f) -> p r f", r=1).broadcast_to([128, nblk, 512]),
                        ALU.add, ALU.mult),
                        reads=[rw, rsdbc], writes=[rw])

            emit_load(jobs[0][6])
            emit_load(jobs[0][0])
            emit_load(jobs[0][4])
            emit_load(jobs[0][5])
            emit_load(jobs[0][1])
            p.barrier()
            if g == 0:
                xnres[1] = [p.res("xng1h0"), p.res("xng1h1")]

                def emit_xn_dma(i):
                    h2, qx = i // 2, i % 2
                    p.dma("gpsimd", f"GA{h2}",
                          xng[:, h2 * 8192 + qx * 4096: h2 * 8192 + (qx + 1) * 4096],
                          xblk[2 + h2][:, qx * 4096:(qx + 1) * 4096],
                          writes=[xnres[1][h2]])

            for qq in range(4):
                for ts in range(4):
                    p.dma("scalar", f"SX{ts}", xtq[:, ts * 512:(ts + 1) * 512],
                          xtok[gt0 + ts * 128: gt0 + (ts + 1) * 128,
                               qq * 512:(qq + 1) * 512], writes=[RD[f"xt{ts}"]])
                for ji, j in enumerate(jobs[qq]):
                    if g == 0 and qq == 0 and ji in (3, 5, 7, 9):
                        emit_xn_dma((ji - 3) // 2)
                    emit_load(j)
                    dst, nblk, rw = j["dst"], j["nblk"], j["rw"]
                    src_t = attn if j["kind"] == "wo" else hidden
                    rsrc = rattn if j["kind"] == "wo" else rhid
                    for ts in range(8):
                        for k in range(nblk):
                            chunk = j["a"] + k
                            laststop = (j["kind"] == "wd" and chunk == HC - 1)
                            do_inc = laststop or (k == nblk - 1 and ts == 7)
                            p.op("tensor", lambda E, ts=ts, chunk=chunk, k=k, src_t=src_t,
                                 dst=dst, st=(j["sgidx"] == 0 and k == 0), sp=laststop: E.matmul(
                                     pb[ts][:, :],
                                     src_t[:, chunk * GT + ts * 128: chunk * GT + (ts + 1) * 128],
                                     dst[:, k * 512:(k + 1) * 512],
                                     start=st, stop=sp),
                                 reads=[rw, rsrc], writes=[bank[ts]],
                                 inc=("SP" if do_inc else None))
                if qq + 1 < 4:
                    emit_load(jobs[qq + 1][0])
                    emit_load(jobs[qq + 1][4])
                    emit_load(jobs[qq + 1][5])
                if qq == 3 and g == 0:
                    for h in range(2):
                        p.dma("gpsimd", f"GQ{h}", war[:, h * 2048:(h + 1) * 2048],
                              wq[h], writes=[rd[f"wo{h}"]])
                    wq_next["res"] = [rd["wo0"], rd["wo1"], rd["wd0"], rd["wd1"]]
                for ts in range(8):
                    tpar = ts % 4
                    rxt, rov = RD[f"xt{tpar}"], RD[f"oev{tpar}"]
                    xt_ap = xtq[:, tpar * 512:(tpar + 1) * 512]
                    ov_ap = oevf[:, tpar * 512:(tpar + 1) * 512]
                    rows = slice(gt0 + ts * 128, gt0 + (ts + 1) * 128)
                    cols = slice(qq * 512, (qq + 1) * 512)
                    p.op("vector", lambda E, ts=ts, xt_ap=xt_ap, ov_ap=ov_ap: E.tensor_tensor(
                        ov_ap, pb[ts][:, :], xt_ap, ALU.add),
                        reads=[bank[ts], rxt], writes=[rov])
                    p.dma("sync", f"SY{tpar}", y[rows, cols], ov_ap, reads=[rov])
                    if ts < 4:
                        p.dma("scalar", f"SX{tpar}", xt_ap,
                              xtok[gt0 + (ts + 4) * 128: gt0 + (ts + 5) * 128, cols],
                              writes=[rxt])
                if g == 0:
                    emit_xn_build(1, dcs=range(qq * 4, qq * 4 + 4))
            p.barrier()

        fin = [(s, p.cnt[s]) for s in DMASEMS if p.cnt[s] > 0]
        p.op("sync", lambda E: E.nop(), extra=fin)

        with nc.Block() as block:
            @block.sync
            def _(E):
                for t in p.th["sync"]:
                    t(E, sems)

            @block.gpsimd
            def _(E):
                for t in p.th["gpsimd"]:
                    t(E, sems)

            @block.tensor
            def _(E):
                for t in p.th["tensor"]:
                    t(E, sems)

            @block.vector
            def _(E):
                for t in p.th["vector"]:
                    t(E, sems)

            @block.scalar
            def _(E):
                for t in p.th["scalar"]:
                    t(E, sems)

    return nc


_NC_CACHE = []


def _get_nc():
    if not _NC_CACHE:
        _NC_CACHE.append(build_nc())
    return _NC_CACHE[0]


def _prep_weights(L0_attn_Wq, L0_attn_bq, L0_attn_Wk, L0_attn_bk, L0_attn_Wv,
                  L0_attn_bv, L0_attn_Wo, L0_ffn_wq_gate, L0_ffn_s_gate,
                  L0_ffn_wq_up, L0_ffn_s_up, L0_ffn_wq_down, L0_ffn_s_down):
    C = np.ascontiguousarray
    Wq = np.asarray(L0_attn_Wq, dtype=np.float32)
    Wk = np.asarray(L0_attn_Wk, dtype=np.float32)
    Wv = np.asarray(L0_attn_Wv, dtype=np.float32)
    Wo = np.asarray(L0_attn_Wo, dtype=np.float32)

    def u8view(a):
        a = np.ascontiguousarray(np.asarray(a))
        assert a.dtype == np.int32
        return C(a.view(np.uint8).reshape(*a.shape, 4)[..., 0])

    wg8 = u8view(L0_ffn_wq_gate)
    wu8 = u8view(L0_ffn_wq_up)
    wd8 = u8view(L0_ffn_wq_down)

    d = {}
    d["wq"] = C(Wq.reshape(H, 128, DC, 128).transpose(0, 3, 2, 1).reshape(H, 128, D))
    Wkv = np.concatenate([Wk, Wv], axis=0)
    d["wkv"] = C(Wkv.reshape(1024, DC, 128).transpose(2, 1, 0))
    d["bkv"] = C(np.concatenate([np.asarray(L0_attn_bk, np.float32),
                                 np.asarray(L0_attn_bv, np.float32)])[None, :])
    d["bq"] = C(np.asarray(L0_attn_bq, np.float32).reshape(H, 128).T)
    d["wg"] = C(wg8.reshape(HC, 128, DC, 128).transpose(0, 3, 2, 1).reshape(HC, 128, D))
    d["wu"] = C(wu8.reshape(HC, 128, DC, 128).transpose(0, 3, 2, 1).reshape(HC, 128, D))
    d["sgi"] = C(np.asarray(L0_ffn_s_gate, np.float32).reshape(HC, 128).T)
    d["sui"] = C(np.asarray(L0_ffn_s_up, np.float32).reshape(HC, 128).T)
    d["wo"] = C(Wo.reshape(4, 512, H, 128).transpose(0, 3, 2, 1).reshape(4, 128, H * 512))
    d["wd"] = C(wd8.reshape(4, 512, HC, 128).transpose(0, 3, 2, 1).reshape(4, 128, HC * 512))
    d["sd"] = C(np.asarray(L0_ffn_s_down, np.float32)[None, :])
    return d


def kernel(L0_attn_hidden_states, L0_attn_Wq, L0_attn_bq, L0_attn_Wk, L0_attn_bk,
           L0_attn_Wv, L0_attn_bv, L0_attn_Wo, L0_ffn_wq_gate, L0_ffn_s_gate,
           L0_ffn_wq_up, L0_ffn_s_up, L0_ffn_wq_down, L0_ffn_s_down):
    x = np.asarray(L0_attn_hidden_states, dtype=np.float32)
    wdict = _prep_weights(L0_attn_Wq, L0_attn_bq, L0_attn_Wk, L0_attn_bk,
                          L0_attn_Wv, L0_attn_bv, L0_attn_Wo, L0_ffn_wq_gate,
                          L0_ffn_s_gate, L0_ffn_wq_up, L0_ffn_s_up,
                          L0_ffn_wq_down, L0_ffn_s_down)
    C = np.ascontiguousarray
    in_maps = []
    for c in range(8):
        b, half = c // 2, c % 2
        own = x[b][half * 2048:(half + 1) * 2048]
        m = dict(wdict)
        m["xblk"] = C(own.reshape(4, 512, DC, 128).transpose(0, 3, 2, 1).reshape(4, 128, 8192))
        m["xtok"] = C(own)
        in_maps.append(m)

    nc = _get_nc()
    res = run_bass_kernel_spmd(nc, in_maps, core_ids=list(range(8)))
    out = np.empty((4, 4096, 2048), dtype=np.float32)
    for c in range(8):
        b, half = c // 2, c % 2
        out[b, half * 2048:(half + 1) * 2048] = res.results[c]["y"]
    return out


if __name__ == "__main__":
    import reference as _r
    ins = _r.setup_inputs()
    got = kernel(**{k: np.asarray(v) for k, v in ins.items()})
    want = np.asarray(_r.reference(**ins))
    num = np.linalg.norm((got - want).ravel())
    den = np.linalg.norm(want.ravel())
    print("rel_err:", num / den)
